# version 31
# baseline (speedup 1.0000x reference)
import numpy as np
import ml_dtypes
import concourse.bass as bass
import concourse.mybir as mybir
from concourse.bass_utils import run_bass_kernel_spmd

F32 = mybir.dt.float32
BF16 = mybir.dt.bfloat16
I32 = mybir.dt.int32
AF = mybir.ActivationFunctionType
ALU = mybir.AluOpType
AX = mybir.AxisListType

D = 1024
FF = 4096
T1 = 32
T2 = 16
NH = 16
NCHIP = 8

MAGIC = 0x5F3759DF


def build_nc():
    nc = bass.Bass("TRN2", target_bir_lowering=False)

    xkv = nc.dram_tensor("xkv", [T1 * 128, D], F32, kind="ExternalInput").ap()
    wqTd = nc.dram_tensor("wqT", [D, D], F32, kind="ExternalInput").ap()
    wkTd = nc.dram_tensor("wkT", [D, D], F32, kind="ExternalInput").ap()
    wvTd = nc.dram_tensor("wvT", [D, D], F32, kind="ExternalInput").ap()
    woTd = nc.dram_tensor("woT", [D, D], F32, kind="ExternalInput").ap()
    wupTd = nc.dram_tensor("wupT", [D, FF], F32, kind="ExternalInput").ap()
    wdnTd = nc.dram_tensor("wdnT", [FF, D], F32, kind="ExternalInput").ap()
    outd = nc.dram_tensor("out", [T2 * 128, D], F32, kind="ExternalOutput").ap()

    identd = nc.inline_tensor(np.eye(128, dtype=ml_dtypes.bfloat16), "identc").ap()
    onecd = nc.inline_tensor(np.ones((128, 1), np.float32), "onecolc").ap()
    onerd = nc.inline_tensor(np.ones((1, 128), np.float32), "onerowc").ap()

    sb = nc.alloc_sbuf_tensor
    ident = sb("ident", [128, 128], BF16).ap()
    onec = sb("onec", [128, 1], F32).ap()
    oner = sb("oner", [1, 128], F32).ap()
    wA = sb("wA", [128, 8 * D], BF16).ap()
    wB = sb("wB", [128, 8 * D], BF16).ap()
    wup2 = sb("wup2", [128, 8 * FF], BF16).ap()
    wdn2 = sb("wdn2", [128, 32 * D], BF16).ap()
    kvsb = sb("kvsb", [128, 512], BF16).ap()
    stats = sb("stats", [128, 2 * T1], F32).ap()
    invstd = sb("invstd", [128, T1], F32).ap()
    lnb = sb("lnb", [128, T1], F32).ap()
    invr2 = sb("invr2", [128, T2], F32).ap()
    bn12 = sb("bn12", [128, 12], F32).ap()
    n1 = sb("n1", [128, 1], F32).ap()
    n2 = sb("n2", [128, 1], F32).ap()
    n3 = sb("n3", [128, 1], F32).ap()
    n4 = sb("n4", [128, 1], F32).ap()
    nb1 = sb("nb1", [128, T1], F32).ap()
    nb2 = sb("nb2", [128, T1], F32).ap()
    nb3 = sb("nb3", [128, T1], F32).ap()
    nb4 = sb("nb4", [128, T1], F32).ap()
    nb5 = sb("nb5", [128, T1], F32).ap()
    thrU = sb("thrU", [128, 1], F32).ap()
    nthrU = sb("nthrU", [128, 1], F32).ap()
    thrD = sb("thrD", [128, 1], F32).ap()
    nthrD = sb("nthrD", [128, 1], F32).ap()
    paU = sb("paU", [128, 32], F32).ap()
    paD = sb("paD", [128, 32], F32).ap()
    alsc = sb("alsc", [1, 1], F32).ap()
    ssq4 = sb("ssq4", [128, 4], F32).ap()
    sct = sb("sct", [128, 1], F32).ap()
    xb0 = sb("xb0", [128, D], F32).ap()
    xb1 = sb("xb1", [128, D], F32).ap()
    af = sb("af", [128, 4 * D], F32).ap()
    ab = sb("ab", [128, 10240], BF16).ap()

    stg = [af[:, 0:2048], af[:, 2048:4096]]
    t1s = af[:, 0:1024]
    hTf = af[:, 1024:2048].bitcast(BF16)
    hTfc = [hTf[:, 0:1024], hTf[:, 1024:2048]]
    gs = af[:, 2048:3072]
    tmp = af[:, 3072:4096]
    outs = af[:, 0:1024]

    xnb = [ab[:, 0:1024], ab[:, 1024:2048]]
    xnTs = [ab[:, 2048:3072], ab[:, 7168:8192]]
    xnT = ab[:, 2048:3072]
    ksb = ab[:, 3072:4096]
    vsb = ab[:, 4096:5120]
    s1b = ab[:, 5120:6144]
    s2b = ab[:, 6144:7168]
    hb = [ab[:, 3072:4096], ab[:, 4096:5120],
          ab[:, 5120:6144], ab[:, 6144:7168]]
    hTb = [ab[:, 7168:7680], ab[:, 7680:8192]]
    qsb = ab[:, 8192:9216]
    qTb = ab[:, 9216:10240]
    aTb = ab[:, 8192:9216]
    dvA = ab[:, 8192:9216]
    dvB = ab[:, 9216:10240]

    p01 = nc.alloc_psum_tensor("p01", [128, 1024], F32).ap()
    p23 = nc.alloc_psum_tensor("p23", [128, 1024], F32).ap()
    p45 = nc.alloc_psum_tensor("p45", [128, 1024], F32).ap()
    p67 = nc.alloc_psum_tensor("p67", [128, 1024], F32).ap()

    sem_names = ["s_pe", "s_dve", "s_act", "s_pool",
                 "s_x0", "s_x1", "s_x2", "s_x3", "s_x4", "s_x5", "s_w0", "s_w1",
                 "s_wq", "s_out", "s_c", "s_t0", "s_t1", "s_tq", "s_th0", "s_th1"]
    sems = {name: nc.ctx.enter_context(nc.semaphore(name)) for name in sem_names}

    ENG = {"vector": "s_dve", "scalar": "s_act", "gpsimd_c": "s_pool"}
    ops = {k: [] for k in ["tensor", "vector", "scalar", "gpsimd", "sync"]}
    cnt = {name: 0 for name in sem_names}
    ev = {}

    def op(eng, fn, inc=None, n=1):
        key = eng
        auto = None
        if eng in ("vector", "scalar"):
            auto = ENG[eng]
        elif eng == "gpsimd_c":
            key = "gpsimd"
            auto = "s_pool"
        elif eng == "scalar_dma":
            key = "scalar"
        if auto is not None:
            ops[key].append(("wait", auto, cnt[auto]))
            ops[key].append(("op", fn, auto, 1))
            cnt[auto] += 1
        else:
            ops[key].append(("op", fn, inc, n))
            if inc is not None:
                cnt[inc] += n

    def wait(eng, sem, val):
        key = {"gpsimd_c": "gpsimd", "scalar_dma": "scalar"}.get(eng, eng)
        ops[key].append(("wait", sem, val))

    def wait_ev(eng, name):
        sem, val = ev[name]
        wait(eng, sem, val)

    def mark(name, sem):
        ev[name] = (sem, cnt[sem])

    def pe_inc():
        kind, fn, inc, n = ops["tensor"][-1]
        assert kind == "op" and inc is None
        ops["tensor"][-1] = ("op", fn, "s_pe", 1)
        cnt["s_pe"] += 1

    def newton_rsqrt(src, dst, tmpa, tmpb, iters=3):
        op("vector", lambda src=src, dst=dst: nc.vector.tensor_scalar(
            dst.bitcast(I32), src.bitcast(I32), 1, None, ALU.logical_shift_right))
        op("vector", lambda dst=dst: nc.vector.tensor_scalar(
            dst.bitcast(I32), dst.bitcast(I32), MAGIC, None, ALU.subtract))
        op("vector", lambda dst=dst: nc.vector.tensor_scalar_mul(
            dst.bitcast(I32), dst.bitcast(I32), -1))
        for _ in range(iters):
            op("vector", lambda dst=dst, tmpa=tmpa: nc.vector.tensor_tensor(
                tmpa, dst, dst, ALU.mult))
            op("vector", lambda tmpa=tmpa, src=src, tmpb=tmpb: nc.vector.tensor_tensor(
                tmpb, tmpa, src, ALU.mult))
            op("vector", lambda tmpb=tmpb: nc.vector.tensor_scalar(
                tmpb, tmpb, -0.5, 1.5, ALU.mult, ALU.add))
            op("vector", lambda dst=dst, tmpb=tmpb: nc.vector.tensor_tensor(
                dst, dst, tmpb, ALU.mult))

    def xpose_dma(dst, src, semname):
        jj = src.shape[1] // 128
        op("scalar_dma", lambda dst=dst, src=src, jj=jj: nc.scalar.dma_start(
            dst.rearrange("p (j f) -> p j f", j=jj),
            src.rearrange("p (j f) -> p j f", j=jj),
            transpose=True), inc=semname, n=16)

    op("sync", lambda: nc.sync.dma_start(ident, identd), inc="s_c", n=16)
    op("sync", lambda: nc.sync.dma_start(onec, onecd), inc="s_c", n=16)
    op("sync", lambda: nc.sync.dma_start(oner, onerd), inc="s_c", n=16)
    xbuf = [xb0, xb1]
    nx = {"s_x0": 0, "s_x1": 0}
    ntp = {"s_t0": 0, "s_t1": 0, "s_tq": 0, "s_th0": 0, "s_th1": 0}

    def load_x(tile, pb):
        op("sync", lambda tile=tile, pb=pb: nc.sync.dma_start(
            xbuf[pb], xkv[128 * tile:128 * (tile + 1), :]), inc="s_x%d" % pb, n=16)
        nx["s_x%d" % pb] += 1

    stage_idx = [0]
    stage_done_ev = {}
    last_sadd = [None]
    gchunk = [0]

    def stage_src(job, buf):
        kind, i = job
        if kind.endswith("up"):
            return (wupTd[128 * (i // 2):128 * (i // 2) + 128,
                          2048 * (i % 2):2048 * (i % 2) + 2048], buf)
        return (wdnTd[256 * i:256 * (i + 1), :].rearrange("(r p) f -> p r f", r=2),
                buf.rearrange("p (r f) -> p r f", r=2))

    def wt_dst(job):
        kind, i = job
        if kind.endswith("up"):
            return wup2[:, FF * (i // 2) + 2048 * (i % 2):FF * (i // 2) + 2048 * (i % 2) + 2048]
        return wdn2[:, 2048 * i:2048 * (i + 1)]

    dvC = ab[:, 5120:7168]
    dvD = ab[:, 8192:10240]

    def emit_chunk(job, style):
        j = stage_idx[0]
        stage_idx[0] += 1
        par = j % 2
        buf = stg[par]
        psem = "s_w%d" % par
        dq = "gpsimd" if j % 2 == 0 else "sync"
        if j - 2 in stage_done_ev:
            wait_ev(dq, stage_done_ev[j - 2])
        if dq == "gpsimd":
            op("gpsimd", lambda job=job, buf=buf: nc.gpsimd.dma_start(
                *reversed(stage_src(job, buf))), inc=psem, n=16)
        else:
            op("sync", lambda job=job, buf=buf: nc.sync.dma_start(
                *reversed(stage_src(job, buf))), inc=psem, n=16)
        nready = 16 * ((j // 2) + 1)
        kind, i = job
        if kind[0] == "A":
            pa = paU if kind == "Aup" else paD
            if style == "dve":
                wait("vector", psem, nready)
                op("vector", lambda buf=buf, pa=pa, i=i: nc.vector.reduce_sum(
                    pa[:, i:i + 1], buf, AX.X, apply_absolute_value=True))
                stage_done_ev[j] = "chk%d" % j
                mark("chk%d" % j, "s_dve")
                return
            wait("scalar", psem, nready)
            op("scalar", lambda buf=buf, pa=pa, i=i: nc.scalar.activation(
                dvC, buf, AF.Abs, accum_out=pa[:, i:i + 1]))
            stage_done_ev[j] = "chk%d" % j
            mark("chk%d" % j, "s_act")
        else:
            thr, nthr = (thrU, nthrU) if kind == "Bup" else (thrD, nthrD)
            dst = wt_dst(job)
            wait("vector", psem, nready)
            wait_ev("vector", "thr_" + kind[1:])
            op("vector", lambda buf=buf, thr=thr: nc.vector.tensor_scalar(
                dvD, buf, thr, 2.0, ALU.is_gt, ALU.mult))
            op("vector", lambda buf=buf, nthr=nthr: nc.vector.tensor_scalar(
                dvC, buf, nthr, 2.0, ALU.is_lt, ALU.mult))
            stage_done_ev[j] = "chk%d" % j
            mark("chk%d" % j, "s_dve")
            op("vector", lambda dst=dst: nc.vector.tensor_tensor(
                dst, dvD, dvC, ALU.subtract))

    def emit_alpha(which):
        pa = paU if which == "up" else paD
        thr, nthr = (thrU, nthrU) if which == "up" else (thrD, nthrD)
        nelem = float(D * FF)
        wait_ev("vector", "passA_%s_done_act" % which)
        wait_ev("vector", "passA_%s_done_dve" % which)
        op("vector", lambda pa=pa: nc.vector.reduce_sum(n1, pa[:, 0:16], AX.X))
        mark("alphared_" + which, "s_dve")
        wait_ev("tensor", "alphared_" + which)
        op("tensor", lambda: nc.tensor.matmul(
            p67[0:1, 0:1], n1, onec, start=True, stop=True))
        pe_inc()
        mark("amm1_" + which, "s_pe")
        wait_ev("vector", "amm1_" + which)
        op("vector", lambda: nc.vector.tensor_copy(alsc, p67[0:1, 0:1]))
        mark("aevac_" + which, "s_dve")
        wait_ev("tensor", "aevac_" + which)
        op("tensor", lambda: nc.tensor.matmul(
            p67[:, 4:5], oner, alsc, start=True, stop=True))
        pe_inc()
        mark("amm2_" + which, "s_pe")
        wait_ev("vector", "amm2_" + which)
        op("vector", lambda thr=thr, nelem=nelem: nc.vector.tensor_scalar_mul(
            thr, p67[:, 4:5], 0.5 / nelem))
        op("vector", lambda thr=thr, nthr=nthr: nc.vector.tensor_scalar_mul(
            nthr, thr, -1.0))
        mark("thr_" + which, "s_dve")

    def emit_params(batch):
        lo = T2 * batch
        wait_ev("vector", "statsA%d" % (lo + T2 - 1))
        S1v = stats[:, 2 * lo:2 * (lo + T2):2]
        S2v = stats[:, 2 * lo + 1:2 * (lo + T2):2]
        iv = invstd[:, lo:lo + T2]
        m_ = nb4[:, 0:T2]
        op("vector", lambda S1v=S1v, m_=m_: nc.vector.tensor_scalar_mul(
            m_, S1v, 1.0 / D))
        op("vector", lambda m_=m_: nc.vector.tensor_tensor(
            nb2[:, 0:T2], m_, m_, ALU.mult))
        op("vector", lambda S2v=S2v: nc.vector.scalar_tensor_tensor(
            nb1[:, 0:T2], S2v, 1.0 / D, nb2[:, 0:T2], ALU.mult, ALU.subtract))
        op("vector", lambda: nc.vector.tensor_scalar_add(
            nb2[:, 0:T2], nb1[:, 0:T2], 1e-5))
        newton_rsqrt(nb2[:, 0:T2], iv, nb3[:, 0:T2], nb5[:, 0:T2])
        op("vector", lambda m_=m_, iv=iv, lo=lo: nc.vector.scalar_tensor_tensor(
            lnb[:, lo:lo + T2], m_, -1.0, iv, ALU.mult, ALU.mult))
        if batch == 0:
            op("vector", lambda: nc.vector.reciprocal(nb3[:, 0:T2], nb2[:, 0:T2]))
            op("vector", lambda: nc.vector.tensor_tensor(
                nb5[:, 0:T2], nb1[:, 0:T2], nb3[:, 0:T2], ALU.mult))
            newton_rsqrt(nb5[:, 0:T2], nb3[:, 0:T2], nb1[:, 0:T2], nb4[:, 0:T2])
            op("vector", lambda: nc.vector.tensor_tensor(
                nb1[:, 0:T2], nb5[:, 0:T2], nb3[:, 0:T2], ALU.mult))
            op("vector", lambda: nc.vector.tensor_scalar_add(
                nb1[:, 0:T2], nb1[:, 0:T2], 1e-8))
            op("vector", lambda: nc.vector.reciprocal(nb4[:, 0:T2], nb1[:, 0:T2]))
            op("vector", lambda: nc.vector.tensor_scalar_mul(invr2, nb4[:, 0:T2], 0.5))
        mark("params%d" % batch, "s_dve")

    passAup = [("Aup", i) for i in range(16)]
    xpre = [wup2[:, 0:2048].bitcast(F32), wup2[:, 2048:4096].bitcast(F32),
            wdn2[:, 0:2048].bitcast(F32), wdn2[:, 2048:4096].bitcast(F32)]
    for k4 in (2, 3, 4, 5):
        nx["s_x%d" % k4] = 0
    for i in range(T1):
        pq4 = i % 4
        psem4 = "s_x%d" % (2 + pq4)
        if i >= 4:
            wait_ev("sync", "statsA%d" % (i - 4))
            wait_ev("sync", "statsV%d" % (i - 4))
        op("sync", lambda i=i, pq4=pq4: nc.sync.dma_start(
            xpre[pq4], xkv[128 * i:128 * (i + 1), :]), inc=psem4, n=16)
        nx[psem4] += 1
        wait("scalar", psem4, 16 * nx[psem4])
        op("scalar", lambda i=i, pq4=pq4: nc.scalar.activation(
            dvC[:, 0:1024], xpre[pq4], AF.Identity, accum_out=stats[:, 2 * i:2 * i + 1]))
        mark("statsA%d" % i, "s_act")
        wait("vector", psem4, 16 * nx[psem4])
        op("vector", lambda i=i, pq4=pq4: nc.vector.scalar_tensor_tensor(
            dvD[:, 0:1024], xpre[pq4], 1.0, xpre[pq4], ALU.mult, ALU.mult,
            accum_out=stats[:, 2 * i + 1:2 * i + 2]))
        mark("statsV%d" % i, "s_dve")
        if i % 2 == 0:
            emit_chunk(passAup[i // 2], "scalar")
        if i == 20:
            for d in range(8):
                op("gpsimd", lambda d=d: nc.gpsimd.dma_start(
                    wA[:, D * d:D * (d + 1)], wkTd[128 * d:128 * (d + 1), :]),
                   inc="s_wq", n=16)
                op("gpsimd", lambda d=d: nc.gpsimd.dma_start(
                    wB[:, D * d:D * (d + 1)], wvTd[128 * d:128 * (d + 1), :]),
                   inc="s_wq", n=16)
        if i == T2 - 1:
            emit_params(0)
    emit_params(1)

    mark("passA_up_done_act", "s_act")
    mark("passA_up_done_dve", "s_dve")
    wait("tensor", "s_c", 48)
    emit_alpha("up")

    flat_jobs = ([("Adn", i) for i in range(16)] + ["ALPHA_DN"] +
                 [("Bup", i) for i in range(16)] + [("Bdn", i) for i in range(16)])
    sweep_jobs = {}
    NSP = 30
    for t in range(T1):
        sweep_jobs[t] = flat_jobs[(49 * min(t, NSP)) // NSP:(49 * min(t + 1, NSP)) // NSP]

    def s2_scalar(t):
        pb = t % 2
        tsem = "s_t%d" % pb
        wait("scalar", "s_x%d" % pb, 16 * nx["s_x%d" % pb])
        wait("scalar", tsem, 16 * ntp[tsem])
        op("scalar", lambda t=t, pb=pb: nc.scalar.activation(
            xnb[pb], xbuf[pb], AF.Identity,
            bias=lnb[:, t:t + 1], scale=invstd[:, t:t + 1]))
        mark("xn2_%d" % t, "s_act")
        if t >= 1:
            wait_ev("scalar_dma", "q_%d" % (t - 1))
        else:
            wait_ev("scalar_dma", "kv%d" % (T1 - 2))
        wait_ev("scalar_dma", "xn2_%d" % t)
        xpose_dma(xnT, xnb[pb], tsem)
        ntp[tsem] += 1

    def s1_load(t):
        q4 = t % 2
        p4 = "s_x%d" % q4
        op("sync", lambda t=t, q4=q4: nc.sync.dma_start(
            xbuf[q4], xkv[128 * t:128 * (t + 1), :]), inc=p4, n=16)
        nx[p4] += 1

    def s1_scalar(t, kv_pe=None):
        pb = t % 2
        q4 = t % 2
        tsem = "s_t%d" % pb
        wait("scalar", "s_x%d" % q4, 16 * nx["s_x%d" % q4])
        wait_ev("scalar", "params%d" % (t // T2))
        wait("scalar", tsem, 16 * ntp[tsem])
        op("scalar", lambda t=t, pb=pb, q4=q4: nc.scalar.activation(
            xnb[pb], xbuf[q4], AF.Identity,
            bias=lnb[:, t:t + 1], scale=invstd[:, t:t + 1]))
        mark("xn%d" % t, "s_act")
        if kv_pe is not None:
            wait("scalar_dma", "s_pe", kv_pe)
        elif t >= 2:
            wait_ev("scalar_dma", "kv%d" % (t - 2))
        wait_ev("scalar_dma", "xn%d" % t)
        xpose_dma(xnTs[pb], xnb[pb], tsem)
        ntp[tsem] += 1

    s1_load(0)
    s1_load(1)
    s1_scalar(0)
    s1_scalar(1)
    wait_ev("sync", "xn0")
    s1_load(2)
    s1_scalar(2, kv_pe=4)

    for t in range(T1):
        pb = t % 2
        tsem = "s_t%d" % pb

        if t == 0:
            wait("tensor", "s_wq", 256)
        wait("tensor", tsem, 16 * ((t // 2) + 1))
        if t >= 1:
            wait_ev("tensor", "kev%d" % (t - 1))
        for d in range(8):
            for c in range(2):
                op("tensor", lambda c=c, d=d, pb=pb: nc.tensor.matmul(
                    p23[:, 512 * c:512 * (c + 1)], xnTs[pb][:, 128 * d:128 * (d + 1)],
                    wA[:, D * d + 512 * c:D * d + 512 * (c + 1)],
                    start=(d == 0), stop=(d == 7)))
        pe_inc()
        mark("k%d" % t, "s_pe")

        if t >= 1:
            wait_ev("tensor", "vev%d" % (t - 1))
            for h in range(NH):
                rb = 64 * (h % 2)
                cb = 512 + 64 * (h // 2)
                op("tensor", lambda h=h, rb=rb, cb=cb, tt=t - 1: nc.tensor.matmul(
                    p67[rb:rb + 64, cb:cb + 64], ksb[:, 64 * h:64 * (h + 1)],
                    vsb[:, 64 * h:64 * (h + 1)],
                    start=(tt == 0 and h <= 1), stop=False,
                    skip_group_check=True))
            pe_inc()
            mark("kvacc%d" % (t - 1), "s_pe")

        if t >= 1:
            wait_ev("tensor", "vev%d" % (t - 1))
        for d in range(8):
            for c in range(2):
                op("tensor", lambda c=c, d=d, pb=pb: nc.tensor.matmul(
                    p45[:, 512 * c:512 * (c + 1)], xnTs[pb][:, 128 * d:128 * (d + 1)],
                    wB[:, D * d + 512 * c:D * d + 512 * (c + 1)],
                    start=(d == 0), stop=(d == 7)))
        pe_inc()
        mark("kv%d" % t, "s_pe")

        if t >= 1:
            wait_ev("scalar", "kvacc%d" % (t - 1))
        else:
            wait_ev("scalar", "k%d" % t)
        op("scalar", lambda: nc.scalar.activation(ksb, p23, AF.Identity))
        mark("kev%d" % t, "s_act")
        wait_ev("scalar", "kv%d" % t)
        op("scalar", lambda: nc.scalar.activation(vsb, p45, AF.Identity))
        mark("vev%d" % t, "s_act")

        if t + 3 < T1:
            wait_ev("sync", "xn%d" % (t + 1))
            s1_load(t + 3)
            nj_t = 2 if "ALPHA_DN" in sweep_jobs.get(t, []) else 0
            s1_scalar(t + 3, kv_pe=cnt["s_pe"] + nj_t + 3)
        if t == 28:
            wait_ev("sync", "xn%d" % (T1 - 2))
            load_x(0, 0)
        if t == 29:
            wait_ev("sync", "xn%d" % (T1 - 1))
            load_x(1, 1)
        if t == 30:
            s2_scalar(0)
        for jb in sweep_jobs.get(t, []):
            if jb == "ALPHA_DN":
                mark("passA_dn_done_act", "s_act")
                mark("passA_dn_done_dve", "s_dve")
                emit_alpha("dn")
            else:
                gidx = gchunk[0]
                gchunk[0] += 1
                emit_chunk(jb, "dve")
                if jb == ("Bup", 15):
                    mark("ternup_done", "s_dve")
                    mark("ternup_done_act", "s_act")
                if jb == ("Bdn", 15):
                    mark("terndn_done", "s_dve")
                    mark("terndn_done_act", "s_act")

    wait_ev("tensor", "vev%d" % (T1 - 1))
    for h in range(NH):
        rb = 64 * (h % 2)
        cb = 512 + 64 * (h // 2)
        op("tensor", lambda h=h, rb=rb, cb=cb: nc.tensor.matmul(
            p67[rb:rb + 64, cb:cb + 64], ksb[:, 64 * h:64 * (h + 1)],
            vsb[:, 64 * h:64 * (h + 1)],
            start=False, stop=(h == NH - 1), skip_group_check=True))
    pe_inc()
    mark("kvacc%d" % (T1 - 1), "s_pe")

    wait_ev("vector", "kvacc%d" % (T1 - 1))
    op("vector", lambda: nc.vector.tensor_scalar_mul(kvsb, p67[:, 512:1024], 0.125))
    mark("kvsb", "s_dve")
    wait("gpsimd", "s_t0", 16 * 17)
    wait_ev("gpsimd", "k%d" % (T1 - 1))
    for c4 in range(4):
        op("gpsimd", lambda c4=c4: nc.gpsimd.dma_start(
            wA[:, 2048 * c4:2048 * (c4 + 1)].rearrange("p (d f) -> p d f", d=2),
            wqTd[256 * c4:256 * (c4 + 1), :].rearrange("(d p) f -> p d f", d=2)),
           inc="s_wq", n=16)
    wait_ev("gpsimd", "kv%d" % (T1 - 1))
    for c4 in range(4):
        op("gpsimd", lambda c4=c4: nc.gpsimd.dma_start(
            wB[:, 2048 * c4:2048 * (c4 + 1)].rearrange("p (d f) -> p d f", d=2),
            woTd[256 * c4:256 * (c4 + 1), :].rearrange("(d p) f -> p d f", d=2)),
           inc="s_wq", n=16)



    nstores = [0]
    for t in range(T2):
        pb = t % 2
        tsem = "s_t%d" % pb

        if t == 0:
            wait_ev("tensor", "ternup_done")
            wait_ev("tensor", "ternup_done_act")
            wait_ev("tensor", "kvsb")
        wait("tensor", tsem, 16 * (T1 // 2 + (t // 2) + 1))

        def emit_hxpose(fc):
            if fc >= 2:
                wait_ev("scalar_dma", "dwn_%d_%d" % (t, fc - 2))
            elif t >= 1:
                wait_ev("scalar_dma", "dwn_%d_%d" % (t - 1, fc + 2))
            wait_ev("scalar_dma", "h2_%d_%d" % (t, fc))
            xpose_dma(hTfc[fc % 2], hb[fc], "s_th%d" % (fc % 2))
            ntp["s_th%d" % (fc % 2)] += 1

        pq = [p23, p45, p23, p45]
        for fc in range(4):
            if fc >= 2:
                wait_ev("tensor", "h2_%d_%d" % (t, fc - 2))
            elif t >= 1:
                if fc == 0:
                    wait_ev("tensor", "aT_%d" % (t - 1))
                else:
                    wait_ev("tensor", "h2_%d_3" % (t - 1))
            pp = pq[fc]
            for d in range(8):
                st = (d == 0)
                sp = (d == 7)
                for c in range(2):
                    op("tensor", lambda pp=pp, d=d, fc=fc, c=c, st=st, sp=sp: nc.tensor.matmul(
                        pp[:, 512 * c:512 * (c + 1)], xnT[:, 128 * d:128 * (d + 1)],
                        wup2[:, FF * d + 1024 * fc + 512 * c:FF * d + 1024 * fc + 512 * (c + 1)],
                        start=st, stop=sp))
            pe_inc()
            mark("upq_%d_%d" % (t, fc), "s_pe")
            wait_ev("scalar", "upq_%d_%d" % (t, fc))
            op("scalar", lambda pp=pp, fc=fc, t=t: nc.scalar.activation(
                hb[fc], pp, AF.Gelu, scale=invr2[:, t:t + 1]))
            mark("h2_%d_%d" % (t, fc), "s_act")
            if fc == 0:
                wait("scalar", tsem, 16 * (T1 // 2 + (t // 2) + 1))
            op("scalar", lambda fc=fc, pb=pb: nc.scalar.activation(
                xnb[pb], hb[fc], AF.Square, accum_out=ssq4[:, fc:fc + 1]))
            mark("sq_%d_%d" % (t, fc), "s_act")
            if fc < 2:
                emit_hxpose(fc)

        if t == 0:
            wait("tensor", "s_wq", 384)
        wait_ev("tensor", "h2_%d_2" % t)
        for d in range(8):
            for c in range(2):
                op("tensor", lambda d=d, c=c: nc.tensor.matmul(
                    p23[:, 512 * c:512 * (c + 1)], xnT[:, 128 * d:128 * (d + 1)],
                    wA[:, D * d + 512 * c:D * d + 512 * (c + 1)],
                    start=(d == 0), stop=(d == 7)))
        pe_inc()
        mark("q_%d" % t, "s_pe")
        wait_ev("vector", "q_%d" % t)
        op("vector", lambda: nc.vector.tensor_copy(qsb, p23))
        mark("qsb_%d" % t, "s_dve")
        if t >= 1:
            wait_ev("scalar_dma", "at_%d" % (t - 1))
        wait_ev("scalar_dma", "qsb_%d" % t)
        xpose_dma(qTb, qsb, "s_tq")
        ntp["s_tq"] += 1

        if t + 1 < T2:
            s2_scalar(t + 1)

        def emit_ssq():
            for fc in range(4):
                wait_ev("vector", "sq_%d_%d" % (t, fc))
            op("vector", lambda: nc.vector.reduce_sum(n1, ssq4, AX.X))
            op("vector", lambda: nc.vector.tensor_scalar_mul(n1, n1, 1.0 / FF))
            newton_rsqrt(n1, n2, n3, n4, iters=2)
            op("vector", lambda: nc.vector.tensor_tensor(n3, n1, n2, ALU.mult))
            op("vector", lambda: nc.vector.tensor_scalar_add(n3, n3, 1e-8))
            op("vector", lambda: nc.vector.reciprocal(n4, n3))
            op("vector", lambda: nc.vector.tensor_scalar_mul(sct, n4, 0.25))
            mark("sct_%d" % t, "s_dve")

        def emit_down(fc):
            if fc == 0:
                if t >= 1:
                    wait_ev("tensor", "g_%d" % (t - 1))
                else:
                    wait_ev("tensor", "terndn_done")
                    wait_ev("tensor", "terndn_done_act")
            wait("tensor", "s_th%d" % (fc % 2), 16 * (2 * t + fc // 2 + 1))
            for k in range(8):
                Fj = 8 * fc + k
                st = (Fj == 0)
                sp = (Fj == 31)
                for c in range(2):
                    op("tensor", lambda Fj=Fj, k=k, fc=fc, c=c, st=st, sp=sp: nc.tensor.matmul(
                        p67[:, 512 * c:512 * (c + 1)],
                        hTfc[fc % 2][:, 128 * k:128 * (k + 1)],
                        wdn2[:, 1024 * Fj + 512 * c:1024 * Fj + 512 * (c + 1)],
                        start=st, stop=sp))
            pe_inc()
            mark("dwn_%d_%d" % (t, fc), "s_pe")

        emit_down(0)
        emit_hxpose(2)
        emit_down(1)
        emit_hxpose(3)
        emit_down(2)
        wait("tensor", "s_tq", 16 * ntp["s_tq"])
        wait_ev("tensor", "qsb_%d" % t)
        for h in range(NH):
            rb = 64 * (h % 2)
            fo = h // 2
            op("tensor", lambda h=h, rb=rb, fo=fo: nc.tensor.matmul(
                p23[rb:rb + 64, 128 * fo:128 * (fo + 1)],
                kvsb[rb:rb + 64, 64 * fo:64 * fo + 64],
                qTb[rb:rb + 64, 128 * fo:128 * (fo + 1)],
                start=True, stop=True))
        pe_inc()
        mark("at_%d" % t, "s_pe")
        wait_ev("vector", "at_%d" % t)
        op("vector", lambda: nc.vector.tensor_copy(aTb, p23))
        mark("aT_%d" % t, "s_dve")
        emit_ssq()
        emit_down(3)
        ev["down_%d" % t] = ev["dwn_%d_%d" % (t, 3)]

        wait_ev("scalar", "down_%d" % t)
        wait_ev("scalar", "sct_%d" % t)
        if t >= 1:
            wait("scalar", "s_out", 16 * nstores[0])
            wait_ev("scalar", "g_%d" % (t - 1))
        op("scalar", lambda: nc.scalar.activation(t1s, p67, AF.Tanh, scale=sct))
        mark("t2_%d" % t, "s_act")
        wait_ev("vector", "t2_%d" % t)
        op("vector", lambda: nc.vector.tensor_scalar_add(t1s, t1s, 1.0))
        op("vector", lambda: nc.vector.scalar_tensor_tensor(
            gs, p67, sct, t1s, ALU.mult, ALU.mult))
        mark("g_%d" % t, "s_dve")

        wait_ev("tensor", "aT_%d" % t)
        wait_ev("tensor", "h2_%d_3" % t)
        if t >= 1:
            wait_ev("tensor", "m1_%d" % (t - 1))
        for fo in range(8):
            st = (fo == 0)
            sp = (fo == 7)
            for c in range(2):
                op("tensor", lambda fo=fo, c=c, st=st, sp=sp: nc.tensor.matmul(
                    p01[:, 512 * c:512 * (c + 1)], aTb[:, 128 * fo:128 * (fo + 1)],
                    wB[:, D * fo + 512 * c:D * fo + 512 * (c + 1)],
                    start=st, stop=sp))
        pe_inc()
        mark("s_%d" % t, "s_pe")

        wait_ev("vector", "s_%d" % t)
        wait_ev("vector", "g_%d" % t)
        if t >= 1:
            wait_ev("vector", "m2_%d" % (t - 1))
        op("vector", lambda: nc.vector.scalar_tensor_tensor(
            tmp, p01, 1.0, gs, ALU.mult, ALU.mult))
        mark("m1_%d" % t, "s_dve")
        wait_ev("gpsimd_c", "m1_%d" % t)
        wait("gpsimd_c", "s_out", 16 * nstores[0])
        op("gpsimd_c", lambda pb=pb: nc.gpsimd.tensor_tensor(outs, tmp, xbuf[pb], ALU.add))
        mark("m2_%d" % t, "s_pool")
        wait_ev("sync", "m2_%d" % t)
        op("sync", lambda t=t: nc.sync.dma_start(
            outd[128 * t:128 * (t + 1), :], outs), inc="s_out", n=16)
        nstores[0] += 1

        if t + 2 < T2:
            load_x(t + 2, pb)

    wait("sync", "s_out", 16 * nstores[0])

    def runner(engobj, key):
        for item in ops[key]:
            if item[0] == "wait":
                _, semname, val = item
                if val > 0:
                    engobj.wait_ge(sems[semname], val)
            else:
                _, fn, inc, n = item
                inst = fn()
                if inc is not None:
                    inst.then_inc(sems[inc], n)

    nc._dbg_ops = {k: [(it[0], it[1], it[2]) if it[0] == "wait" else
                        ("op", it[1].__code__.co_firstlineno, it[2])
                       for it in v] for k, v in ops.items()}

    with nc.Block() as block:
        @block.sync
        def _(sync):
            runner(sync, "sync")

        @block.gpsimd
        def _(gpsimd):
            runner(gpsimd, "gpsimd")

        @block.vector
        def _(vector):
            runner(vector, "vector")

        @block.scalar
        def _(scalar):
            runner(scalar, "scalar")

        @block.tensor
        def _(tensor):
            runner(tensor, "tensor")

    return nc


_NC_CACHE = []


def kernel(L0_attn_x, L0_attn_W_q, L0_attn_W_k, L0_attn_W_v, L0_attn_W_o,
           L0_ffn_w_up, L0_ffn_w_down):
    x = np.ascontiguousarray(np.asarray(L0_attn_x, dtype=np.float32))
    wqT = np.ascontiguousarray(np.asarray(L0_attn_W_q, dtype=np.float32).T)
    wkT = np.ascontiguousarray(np.asarray(L0_attn_W_k, dtype=np.float32).T)
    wvT = np.ascontiguousarray(np.asarray(L0_attn_W_v, dtype=np.float32).T)
    woT = np.ascontiguousarray(np.asarray(L0_attn_W_o, dtype=np.float32).T)
    wupT = np.ascontiguousarray(np.asarray(L0_ffn_w_up, dtype=np.float32).T)
    wdnT = np.ascontiguousarray(np.asarray(L0_ffn_w_down, dtype=np.float32).T)

    in_maps = []
    for c in range(NCHIP):
        b, half = c // 2, c % 2
        xb = x[b]
        own = xb[2048 * half:2048 * (half + 1)]
        oth = xb[2048 * (1 - half):2048 * (2 - half)]
        xkv = np.ascontiguousarray(np.concatenate([own, oth], axis=0))
        in_maps.append({"xkv": xkv, "wqT": wqT, "wkT": wkT, "wvT": wvT,
                        "woT": woT, "wupT": wupT, "wdnT": wdnT})

    if not _NC_CACHE:
        _NC_CACHE.append(build_nc())
    nc = _NC_CACHE[0]
    res = run_bass_kernel_spmd(nc, in_maps, core_ids=list(range(NCHIP)))

    out = np.empty((4, 4096, 1024), np.float32)
    for c in range(NCHIP):
        b, half = c // 2, c % 2
        out[b, 2048 * half:2048 * (half + 1)] = res.results[c]["out"]
    return out


# version 33
# speedup vs baseline: 1.0743x; 1.0743x over previous
import numpy as np
import ml_dtypes
import concourse.bass as bass
import concourse.mybir as mybir
from concourse.bass_utils import run_bass_kernel_spmd

F32 = mybir.dt.float32
BF16 = mybir.dt.bfloat16
I32 = mybir.dt.int32
AF = mybir.ActivationFunctionType
ALU = mybir.AluOpType
AX = mybir.AxisListType

D = 1024
FF = 4096
T1 = 32
T2 = 16
NH = 16
NCHIP = 8

MAGIC = 0x5F3759DF


def build_nc():
    nc = bass.Bass("TRN2", target_bir_lowering=False)

    xkv = nc.dram_tensor("xkv", [T1 * 128, D], F32, kind="ExternalInput").ap()
    wqTd = nc.dram_tensor("wqT", [D, D], F32, kind="ExternalInput").ap()
    wkTd = nc.dram_tensor("wkT", [D, D], F32, kind="ExternalInput").ap()
    wvTd = nc.dram_tensor("wvT", [D, D], F32, kind="ExternalInput").ap()
    woTd = nc.dram_tensor("woT", [D, D], F32, kind="ExternalInput").ap()
    wupTd = nc.dram_tensor("wupT", [D, FF], F32, kind="ExternalInput").ap()
    wdnTd = nc.dram_tensor("wdnT", [FF, D], F32, kind="ExternalInput").ap()
    outd = nc.dram_tensor("out", [T2 * 128, D], F32, kind="ExternalOutput").ap()

    identd = nc.inline_tensor(np.eye(128, dtype=ml_dtypes.bfloat16), "identc").ap()
    onecd = nc.inline_tensor(np.ones((128, 1), np.float32), "onecolc").ap()
    onerd = nc.inline_tensor(np.ones((1, 128), np.float32), "onerowc").ap()

    sb = nc.alloc_sbuf_tensor
    ident = sb("ident", [128, 128], BF16).ap()
    onec = sb("onec", [128, 1], F32).ap()
    oner = sb("oner", [1, 128], F32).ap()
    wA = sb("wA", [128, 8 * D], BF16).ap()
    wB = sb("wB", [128, 8 * D], BF16).ap()
    wup2 = sb("wup2", [128, 8 * FF], BF16).ap()
    wdn2 = sb("wdn2", [128, 32 * D], BF16).ap()
    kvsb = sb("kvsb", [128, 512], BF16).ap()
    stats = sb("stats", [128, 2 * T1], F32).ap()
    invstd = sb("invstd", [128, T1], F32).ap()
    lnb = sb("lnb", [128, T1], F32).ap()
    invr2 = sb("invr2", [128, T2], F32).ap()
    bn12 = sb("bn12", [128, 12], F32).ap()
    n1 = sb("n1", [128, 1], F32).ap()
    n2 = sb("n2", [128, 1], F32).ap()
    n3 = sb("n3", [128, 1], F32).ap()
    n4 = sb("n4", [128, 1], F32).ap()
    nb1 = sb("nb1", [128, T1], F32).ap()
    nb2 = sb("nb2", [128, T1], F32).ap()
    nb3 = sb("nb3", [128, T1], F32).ap()
    nb4 = sb("nb4", [128, T1], F32).ap()
    nb5 = sb("nb5", [128, T1], F32).ap()
    thrU = sb("thrU", [128, 1], F32).ap()
    nthrU = sb("nthrU", [128, 1], F32).ap()
    thrD = sb("thrD", [128, 1], F32).ap()
    nthrD = sb("nthrD", [128, 1], F32).ap()
    paU = sb("paU", [128, 32], F32).ap()
    paD = sb("paD", [128, 32], F32).ap()
    alsc = sb("alsc", [1, 1], F32).ap()
    ssq4 = sb("ssq4", [128, 4], F32).ap()
    sct = sb("sct", [128, 1], F32).ap()
    xb0 = sb("xb0", [128, D], F32).ap()
    xb1 = sb("xb1", [128, D], F32).ap()
    af = sb("af", [128, 4 * D], F32).ap()
    ab = sb("ab", [128, 10240], BF16).ap()

    stg = [af[:, 0:2048], af[:, 2048:4096]]
    t1s = af[:, 0:1024]
    hTf = af[:, 1024:2048].bitcast(BF16)
    hTfc = [hTf[:, 0:1024], hTf[:, 1024:2048]]
    gs = af[:, 2048:3072]
    tmp = af[:, 3072:4096]
    outs = af[:, 0:1024]

    xnb = [ab[:, 0:1024], ab[:, 1024:2048]]
    xnTs = [ab[:, 2048:3072], ab[:, 7168:8192]]
    xnTs3 = [xnTs[0], xnTs[1], wdn2[:, 31744:32768]]
    xnT = ab[:, 2048:3072]
    ksb = ab[:, 3072:4096]
    vsb = ab[:, 4096:5120]
    s1b = ab[:, 5120:6144]
    s2b = ab[:, 6144:7168]
    hb = [ab[:, 3072:4096], ab[:, 4096:5120],
          ab[:, 5120:6144], ab[:, 6144:7168]]
    hTb = [ab[:, 7168:7680], ab[:, 7680:8192]]
    qsb = ab[:, 8192:9216]
    qTb = ab[:, 9216:10240]
    aTb = ab[:, 8192:9216]
    dvA = ab[:, 8192:9216]
    dvB = ab[:, 9216:10240]

    p01 = nc.alloc_psum_tensor("p01", [128, 1024], F32).ap()
    p23 = nc.alloc_psum_tensor("p23", [128, 1024], F32).ap()
    p45 = nc.alloc_psum_tensor("p45", [128, 1024], F32).ap()
    p67 = nc.alloc_psum_tensor("p67", [128, 1024], F32).ap()

    sem_names = ["s_pe", "s_dve", "s_act", "s_pool",
                 "s_x0", "s_x1", "s_x2", "s_x3", "s_x4", "s_x5", "s_w0", "s_w1",
                 "s_wq", "s_out", "s_c", "s_t0", "s_t1", "s_tq", "s_th0", "s_th1"]
    sems = {name: nc.ctx.enter_context(nc.semaphore(name)) for name in sem_names}

    ENG = {"vector": "s_dve", "scalar": "s_act", "gpsimd_c": "s_pool"}
    ops = {k: [] for k in ["tensor", "vector", "scalar", "gpsimd", "sync"]}
    cnt = {name: 0 for name in sem_names}
    ev = {}

    def op(eng, fn, inc=None, n=1):
        key = eng
        auto = None
        if eng in ("vector", "scalar"):
            auto = ENG[eng]
        elif eng == "gpsimd_c":
            key = "gpsimd"
            auto = "s_pool"
        elif eng == "scalar_dma":
            key = "scalar"
        if auto is not None:
            ops[key].append(("wait", auto, cnt[auto]))
            ops[key].append(("op", fn, auto, 1))
            cnt[auto] += 1
        else:
            ops[key].append(("op", fn, inc, n))
            if inc is not None:
                cnt[inc] += n

    def wait(eng, sem, val):
        key = {"gpsimd_c": "gpsimd", "scalar_dma": "scalar"}.get(eng, eng)
        ops[key].append(("wait", sem, val))

    def wait_ev(eng, name):
        sem, val = ev[name]
        wait(eng, sem, val)

    def mark(name, sem):
        ev[name] = (sem, cnt[sem])

    def pe_inc():
        kind, fn, inc, n = ops["tensor"][-1]
        assert kind == "op" and inc is None
        ops["tensor"][-1] = ("op", fn, "s_pe", 1)
        cnt["s_pe"] += 1

    def newton_rsqrt(src, dst, tmpa, tmpb, iters=3):
        op("vector", lambda src=src, dst=dst: nc.vector.tensor_scalar(
            dst.bitcast(I32), src.bitcast(I32), 1, None, ALU.logical_shift_right))
        op("vector", lambda dst=dst: nc.vector.tensor_scalar(
            dst.bitcast(I32), dst.bitcast(I32), MAGIC, None, ALU.subtract))
        op("vector", lambda dst=dst: nc.vector.tensor_scalar_mul(
            dst.bitcast(I32), dst.bitcast(I32), -1))
        for _ in range(iters):
            op("vector", lambda dst=dst, tmpa=tmpa: nc.vector.tensor_tensor(
                tmpa, dst, dst, ALU.mult))
            op("vector", lambda tmpa=tmpa, src=src, tmpb=tmpb: nc.vector.tensor_tensor(
                tmpb, tmpa, src, ALU.mult))
            op("vector", lambda tmpb=tmpb: nc.vector.tensor_scalar(
                tmpb, tmpb, -0.5, 1.5, ALU.mult, ALU.add))
            op("vector", lambda dst=dst, tmpb=tmpb: nc.vector.tensor_tensor(
                dst, dst, tmpb, ALU.mult))

    def xpose_dma(dst, src, semname):
        jj = src.shape[1] // 128
        op("scalar_dma", lambda dst=dst, src=src, jj=jj: nc.scalar.dma_start(
            dst.rearrange("p (j f) -> p j f", j=jj),
            src.rearrange("p (j f) -> p j f", j=jj),
            transpose=True), inc=semname, n=16)

    op("sync", lambda: nc.sync.dma_start(ident, identd), inc="s_c", n=16)
    op("sync", lambda: nc.sync.dma_start(onec, onecd), inc="s_c", n=16)
    op("sync", lambda: nc.sync.dma_start(oner, onerd), inc="s_c", n=16)
    xbuf = [xb0, xb1]
    nx = {"s_x0": 0, "s_x1": 0}
    ntp = {"s_t0": 0, "s_t1": 0, "s_tq": 0, "s_th0": 0, "s_th1": 0}

    def load_x(tile, pb):
        op("sync", lambda tile=tile, pb=pb: nc.sync.dma_start(
            xbuf[pb], xkv[128 * tile:128 * (tile + 1), :]), inc="s_x%d" % pb, n=16)
        nx["s_x%d" % pb] += 1

    stage_idx = [0]
    stage_done_ev = {}
    last_sadd = [None]
    gchunk = [0]

    def stage_src(job, buf):
        kind, i = job
        if kind.endswith("up"):
            return (wupTd[128 * (i // 2):128 * (i // 2) + 128,
                          2048 * (i % 2):2048 * (i % 2) + 2048], buf)
        return (wdnTd[256 * i:256 * (i + 1), :].rearrange("(r p) f -> p r f", r=2),
                buf.rearrange("p (r f) -> p r f", r=2))

    def wt_dst(job):
        kind, i = job
        if kind.endswith("up"):
            return wup2[:, FF * (i // 2) + 2048 * (i % 2):FF * (i // 2) + 2048 * (i % 2) + 2048]
        return wdn2[:, 2048 * i:2048 * (i + 1)]

    dvC = ab[:, 5120:7168]
    dvD = ab[:, 8192:10240]

    def emit_chunk(job, style):
        j = stage_idx[0]
        stage_idx[0] += 1
        par = j % 2
        buf = stg[par]
        psem = "s_w%d" % par
        dq = "gpsimd" if j % 2 == 0 else "sync"
        if j - 2 in stage_done_ev:
            wait_ev(dq, stage_done_ev[j - 2])
        if dq == "gpsimd":
            op("gpsimd", lambda job=job, buf=buf: nc.gpsimd.dma_start(
                *reversed(stage_src(job, buf))), inc=psem, n=16)
        else:
            op("sync", lambda job=job, buf=buf: nc.sync.dma_start(
                *reversed(stage_src(job, buf))), inc=psem, n=16)
        nready = 16 * ((j // 2) + 1)
        kind, i = job
        if kind[0] == "A":
            pa = paU if kind == "Aup" else paD
            if style == "dve":
                wait("vector", psem, nready)
                op("vector", lambda buf=buf, pa=pa, i=i: nc.vector.reduce_sum(
                    pa[:, i:i + 1], buf, AX.X, apply_absolute_value=True))
                stage_done_ev[j] = "chk%d" % j
                mark("chk%d" % j, "s_dve")
                return
            wait("scalar", psem, nready)
            op("scalar", lambda buf=buf, pa=pa, i=i: nc.scalar.activation(
                dvC, buf, AF.Abs, accum_out=pa[:, i:i + 1]))
            stage_done_ev[j] = "chk%d" % j
            mark("chk%d" % j, "s_act")
        else:
            thr, nthr = (thrU, nthrU) if kind == "Bup" else (thrD, nthrD)
            dst = wt_dst(job)
            wait("vector", psem, nready)
            wait_ev("vector", "thr_" + kind[1:])
            op("vector", lambda buf=buf, thr=thr: nc.vector.tensor_scalar(
                dvD, buf, thr, 2.0, ALU.is_gt, ALU.mult))
            op("vector", lambda buf=buf, nthr=nthr: nc.vector.tensor_scalar(
                dvC, buf, nthr, 2.0, ALU.is_lt, ALU.mult))
            stage_done_ev[j] = "chk%d" % j
            mark("chk%d" % j, "s_dve")
            op("vector", lambda dst=dst: nc.vector.tensor_tensor(
                dst, dvD, dvC, ALU.subtract))

    def emit_alpha(which):
        pa = paU if which == "up" else paD
        thr, nthr = (thrU, nthrU) if which == "up" else (thrD, nthrD)
        nelem = float(D * FF)
        wait_ev("vector", "passA_%s_done_act" % which)
        wait_ev("vector", "passA_%s_done_dve" % which)
        op("vector", lambda pa=pa: nc.vector.reduce_sum(n1, pa[:, 0:16], AX.X))
        mark("alphared_" + which, "s_dve")
        wait_ev("tensor", "alphared_" + which)
        op("tensor", lambda: nc.tensor.matmul(
            p67[0:1, 0:1], n1, onec, start=True, stop=True))
        pe_inc()
        mark("amm1_" + which, "s_pe")
        wait_ev("vector", "amm1_" + which)
        op("vector", lambda: nc.vector.tensor_copy(alsc, p67[0:1, 0:1]))
        mark("aevac_" + which, "s_dve")
        wait_ev("tensor", "aevac_" + which)
        op("tensor", lambda: nc.tensor.matmul(
            p67[:, 4:5], oner, alsc, start=True, stop=True))
        pe_inc()
        mark("amm2_" + which, "s_pe")
        wait_ev("vector", "amm2_" + which)
        op("vector", lambda thr=thr, nelem=nelem: nc.vector.tensor_scalar_mul(
            thr, p67[:, 4:5], 0.5 / nelem))
        op("vector", lambda thr=thr, nthr=nthr: nc.vector.tensor_scalar_mul(
            nthr, thr, -1.0))
        mark("thr_" + which, "s_dve")

    def emit_params(batch):
        lo = T2 * batch
        wait_ev("vector", "statsA%d" % (lo + T2 - 1))
        S1v = stats[:, 2 * lo:2 * (lo + T2):2]
        S2v = stats[:, 2 * lo + 1:2 * (lo + T2):2]
        iv = invstd[:, lo:lo + T2]
        m_ = nb4[:, 0:T2]
        op("vector", lambda S1v=S1v, m_=m_: nc.vector.tensor_scalar_mul(
            m_, S1v, 1.0 / D))
        op("vector", lambda m_=m_: nc.vector.tensor_tensor(
            nb2[:, 0:T2], m_, m_, ALU.mult))
        op("vector", lambda S2v=S2v: nc.vector.scalar_tensor_tensor(
            nb1[:, 0:T2], S2v, 1.0 / D, nb2[:, 0:T2], ALU.mult, ALU.subtract))
        op("vector", lambda: nc.vector.tensor_scalar_add(
            nb2[:, 0:T2], nb1[:, 0:T2], 1e-5))
        newton_rsqrt(nb2[:, 0:T2], iv, nb3[:, 0:T2], nb5[:, 0:T2])
        op("vector", lambda m_=m_, iv=iv, lo=lo: nc.vector.scalar_tensor_tensor(
            lnb[:, lo:lo + T2], m_, -1.0, iv, ALU.mult, ALU.mult))
        if batch == 0:
            op("vector", lambda: nc.vector.reciprocal(nb3[:, 0:T2], nb2[:, 0:T2]))
            op("vector", lambda: nc.vector.tensor_tensor(
                nb5[:, 0:T2], nb1[:, 0:T2], nb3[:, 0:T2], ALU.mult))
            newton_rsqrt(nb5[:, 0:T2], nb3[:, 0:T2], nb1[:, 0:T2], nb4[:, 0:T2])
            op("vector", lambda: nc.vector.tensor_tensor(
                nb1[:, 0:T2], nb5[:, 0:T2], nb3[:, 0:T2], ALU.mult))
            op("vector", lambda: nc.vector.tensor_scalar_add(
                nb1[:, 0:T2], nb1[:, 0:T2], 1e-8))
            op("vector", lambda: nc.vector.reciprocal(nb4[:, 0:T2], nb1[:, 0:T2]))
            op("vector", lambda: nc.vector.tensor_scalar_mul(invr2, nb4[:, 0:T2], 0.5))
        mark("params%d" % batch, "s_dve")

    passAup = [("Aup", i) for i in range(16)]
    xpre = [wup2[:, 0:2048].bitcast(F32), wup2[:, 2048:4096].bitcast(F32),
            wdn2[:, 0:2048].bitcast(F32), wdn2[:, 2048:4096].bitcast(F32)]
    for k4 in (2, 3, 4, 5):
        nx["s_x%d" % k4] = 0
    for i in range(T1):
        pq4 = i % 4
        psem4 = "s_x%d" % (2 + pq4)
        if i >= 4:
            wait_ev("sync", "statsA%d" % (i - 4))
            wait_ev("sync", "statsV%d" % (i - 4))
        op("sync", lambda i=i, pq4=pq4: nc.sync.dma_start(
            xpre[pq4], xkv[128 * i:128 * (i + 1), :]), inc=psem4, n=16)
        nx[psem4] += 1
        wait("scalar", psem4, 16 * nx[psem4])
        op("scalar", lambda i=i, pq4=pq4: nc.scalar.activation(
            dvC[:, 0:1024], xpre[pq4], AF.Identity, accum_out=stats[:, 2 * i:2 * i + 1]))
        mark("statsA%d" % i, "s_act")
        wait("vector", psem4, 16 * nx[psem4])
        op("vector", lambda i=i, pq4=pq4: nc.vector.scalar_tensor_tensor(
            dvD[:, 0:1024], xpre[pq4], 1.0, xpre[pq4], ALU.mult, ALU.mult,
            accum_out=stats[:, 2 * i + 1:2 * i + 2]))
        mark("statsV%d" % i, "s_dve")
        if i % 2 == 0:
            emit_chunk(passAup[i // 2], "scalar")
        if i == 20:
            for d in range(8):
                op("gpsimd", lambda d=d: nc.gpsimd.dma_start(
                    wA[:, D * d:D * (d + 1)], wkTd[128 * d:128 * (d + 1), :]),
                   inc="s_wq", n=16)
                op("gpsimd", lambda d=d: nc.gpsimd.dma_start(
                    wB[:, D * d:D * (d + 1)], wvTd[128 * d:128 * (d + 1), :]),
                   inc="s_wq", n=16)
        if i == T2 - 1:
            emit_params(0)
    emit_params(1)

    mark("passA_up_done_act", "s_act")
    mark("passA_up_done_dve", "s_dve")
    wait("tensor", "s_c", 48)
    emit_alpha("up")

    flat_jobs = ([("Adn", i) for i in range(16)] + ["ALPHA_DN"] +
                 [("Bup", i) for i in range(16)] + [("Bdn", i) for i in range(16)])
    sweep_jobs = {}
    NSP = 30
    for t in range(T1):
        sweep_jobs[t] = flat_jobs[(49 * min(t, NSP)) // NSP:(49 * min(t + 1, NSP)) // NSP]

    def s2_scalar(t):
        pb = t % 2
        tsem = "s_t%d" % pb
        wait("scalar", "s_x%d" % pb, 16 * nx["s_x%d" % pb])
        wait("scalar", tsem, 16 * ntp[tsem])
        op("scalar", lambda t=t, pb=pb: nc.scalar.activation(
            xnb[pb], xbuf[pb], AF.Identity,
            bias=lnb[:, t:t + 1], scale=invstd[:, t:t + 1]))
        mark("xn2_%d" % t, "s_act")
        if t >= 1:
            wait_ev("scalar_dma", "q_%d" % (t - 1))
        else:
            wait_ev("scalar_dma", "kv%d" % (T1 - 2))
        wait_ev("scalar_dma", "xn2_%d" % t)
        xpose_dma(xnT, xnb[pb], tsem)
        ntp[tsem] += 1

    def s1_load(t):
        q4 = t % 2
        p4 = "s_x%d" % q4
        op("sync", lambda t=t, q4=q4: nc.sync.dma_start(
            xbuf[q4], xkv[128 * t:128 * (t + 1), :]), inc=p4, n=16)
        nx[p4] += 1

    def s1_scalar(t):
        pb = t % 2
        q4 = t % 2
        tsem = "s_t%d" % pb
        wait("scalar", "s_x%d" % q4, 16 * nx["s_x%d" % q4])
        wait_ev("scalar", "params%d" % (t // T2))
        wait("scalar", tsem, 16 * ntp[tsem])
        op("scalar", lambda t=t, pb=pb, q4=q4: nc.scalar.activation(
            xnb[pb], xbuf[q4], AF.Identity,
            bias=lnb[:, t:t + 1], scale=invstd[:, t:t + 1]))
        mark("xn%d" % t, "s_act")
        if t >= 3:
            wait_ev("scalar_dma", "kv%d" % (t - 3))
        wait_ev("scalar_dma", "xn%d" % t)
        xpose_dma(xnTs3[t % 3], xnb[pb], tsem)
        ntp[tsem] += 1

    s1_load(0)
    s1_load(1)
    s1_scalar(0)
    s1_scalar(1)

    for t in range(T1):
        pb = t % 2
        tsem = "s_t%d" % pb

        if t == 0:
            wait("tensor", "s_wq", 256)
        wait("tensor", tsem, 16 * ((t // 2) + 1))
        if t >= 1:
            wait_ev("tensor", "kev%d" % (t - 1))
        xnp = xnTs3[t % 3]
        for d in range(8):
            for c in range(2):
                op("tensor", lambda c=c, d=d, xnp=xnp: nc.tensor.matmul(
                    p23[:, 512 * c:512 * (c + 1)], xnp[:, 128 * d:128 * (d + 1)],
                    wA[:, D * d + 512 * c:D * d + 512 * (c + 1)],
                    start=(d == 0), stop=(d == 7)))
        pe_inc()
        mark("k%d" % t, "s_pe")

        if t >= 1:
            wait_ev("tensor", "vev%d" % (t - 1))
            for h in range(NH):
                rb = 64 * (h % 2)
                cb = 512 + 64 * (h // 2)
                op("tensor", lambda h=h, rb=rb, cb=cb, tt=t - 1: nc.tensor.matmul(
                    p67[rb:rb + 64, cb:cb + 64], ksb[:, 64 * h:64 * (h + 1)],
                    vsb[:, 64 * h:64 * (h + 1)],
                    start=(tt == 0 and h <= 1), stop=False,
                    skip_group_check=True))
            pe_inc()
            mark("kvacc%d" % (t - 1), "s_pe")

        if t >= 1:
            wait_ev("tensor", "vev%d" % (t - 1))
        for d in range(8):
            for c in range(2):
                op("tensor", lambda c=c, d=d, xnp=xnp: nc.tensor.matmul(
                    p45[:, 512 * c:512 * (c + 1)], xnp[:, 128 * d:128 * (d + 1)],
                    wB[:, D * d + 512 * c:D * d + 512 * (c + 1)],
                    start=(d == 0), stop=(d == 7)))
        pe_inc()
        mark("kv%d" % t, "s_pe")

        if t >= 1:
            wait_ev("scalar", "kvacc%d" % (t - 1))
        else:
            wait_ev("scalar", "k%d" % t)
        op("scalar", lambda: nc.scalar.activation(ksb, p23, AF.Identity))
        mark("kev%d" % t, "s_act")
        wait_ev("scalar", "kv%d" % t)
        op("scalar", lambda: nc.scalar.activation(vsb, p45, AF.Identity))
        mark("vev%d" % t, "s_act")

        if t + 2 < T1:
            wait_ev("sync", "xn%d" % t)
            s1_load(t + 2)
            s1_scalar(t + 2)
        if t == 28:
            wait_ev("sync", "xn%d" % (T1 - 2))
            load_x(0, 0)
        if t == 29:
            wait_ev("sync", "xn%d" % (T1 - 1))
            load_x(1, 1)
        if t == 30:
            s2_scalar(0)
        for jb in sweep_jobs.get(t, []):
            if jb == "ALPHA_DN":
                mark("passA_dn_done_act", "s_act")
                mark("passA_dn_done_dve", "s_dve")
                emit_alpha("dn")
            else:
                gidx = gchunk[0]
                gchunk[0] += 1
                if jb == ("Bdn", 15):
                    wait_ev("vector", "kv29")
                emit_chunk(jb, "dve")
                if jb == ("Bup", 15):
                    mark("ternup_done", "s_dve")
                    mark("ternup_done_act", "s_act")
                if jb == ("Bdn", 15):
                    mark("terndn_done", "s_dve")
                    mark("terndn_done_act", "s_act")

    wait_ev("tensor", "vev%d" % (T1 - 1))
    for h in range(NH):
        rb = 64 * (h % 2)
        cb = 512 + 64 * (h // 2)
        op("tensor", lambda h=h, rb=rb, cb=cb: nc.tensor.matmul(
            p67[rb:rb + 64, cb:cb + 64], ksb[:, 64 * h:64 * (h + 1)],
            vsb[:, 64 * h:64 * (h + 1)],
            start=False, stop=(h == NH - 1), skip_group_check=True))
    pe_inc()
    mark("kvacc%d" % (T1 - 1), "s_pe")

    wait_ev("vector", "kvacc%d" % (T1 - 1))
    op("vector", lambda: nc.vector.tensor_scalar_mul(kvsb, p67[:, 512:1024], 0.125))
    mark("kvsb", "s_dve")
    wait("gpsimd", "s_t0", 16 * 17)
    wait_ev("gpsimd", "k%d" % (T1 - 1))
    for c4 in range(4):
        op("gpsimd", lambda c4=c4: nc.gpsimd.dma_start(
            wA[:, 2048 * c4:2048 * (c4 + 1)].rearrange("p (d f) -> p d f", d=2),
            wqTd[256 * c4:256 * (c4 + 1), :].rearrange("(d p) f -> p d f", d=2)),
           inc="s_wq", n=16)
    wait_ev("gpsimd", "kv%d" % (T1 - 1))
    for c4 in range(4):
        op("gpsimd", lambda c4=c4: nc.gpsimd.dma_start(
            wB[:, 2048 * c4:2048 * (c4 + 1)].rearrange("p (d f) -> p d f", d=2),
            woTd[256 * c4:256 * (c4 + 1), :].rearrange("(d p) f -> p d f", d=2)),
           inc="s_wq", n=16)



    nstores = [0]
    for t in range(T2):
        pb = t % 2
        tsem = "s_t%d" % pb

        if t == 0:
            wait_ev("tensor", "ternup_done")
            wait_ev("tensor", "ternup_done_act")
            wait_ev("tensor", "kvsb")
        wait("tensor", tsem, 16 * (T1 // 2 + (t // 2) + 1))

        def emit_hxpose(fc):
            if fc >= 2:
                wait_ev("scalar_dma", "dwn_%d_%d" % (t, fc - 2))
            elif t >= 1:
                wait_ev("scalar_dma", "dwn_%d_%d" % (t - 1, fc + 2))
            wait_ev("scalar_dma", "h2_%d_%d" % (t, fc))
            xpose_dma(hTfc[fc % 2], hb[fc], "s_th%d" % (fc % 2))
            ntp["s_th%d" % (fc % 2)] += 1

        pq = [p23, p45, p23, p45]
        for fc in range(4):
            if fc >= 2:
                wait_ev("tensor", "h2_%d_%d" % (t, fc - 2))
            elif t >= 1:
                if fc == 0:
                    wait_ev("tensor", "aT_%d" % (t - 1))
                else:
                    wait_ev("tensor", "h2_%d_3" % (t - 1))
            pp = pq[fc]
            for d in range(8):
                st = (d == 0)
                sp = (d == 7)
                for c in range(2):
                    op("tensor", lambda pp=pp, d=d, fc=fc, c=c, st=st, sp=sp: nc.tensor.matmul(
                        pp[:, 512 * c:512 * (c + 1)], xnT[:, 128 * d:128 * (d + 1)],
                        wup2[:, FF * d + 1024 * fc + 512 * c:FF * d + 1024 * fc + 512 * (c + 1)],
                        start=st, stop=sp))
            pe_inc()
            mark("upq_%d_%d" % (t, fc), "s_pe")
            wait_ev("scalar", "upq_%d_%d" % (t, fc))
            op("scalar", lambda pp=pp, fc=fc, t=t: nc.scalar.activation(
                hb[fc], pp, AF.Gelu, scale=invr2[:, t:t + 1]))
            mark("h2_%d_%d" % (t, fc), "s_act")
            if fc == 0:
                wait("scalar", tsem, 16 * (T1 // 2 + (t // 2) + 1))
            op("scalar", lambda fc=fc, pb=pb: nc.scalar.activation(
                xnb[pb], hb[fc], AF.Square, accum_out=ssq4[:, fc:fc + 1]))
            mark("sq_%d_%d" % (t, fc), "s_act")
            if fc < 2:
                emit_hxpose(fc)

        if t == 0:
            wait("tensor", "s_wq", 384)
        wait_ev("tensor", "h2_%d_2" % t)
        for d in range(8):
            for c in range(2):
                op("tensor", lambda d=d, c=c: nc.tensor.matmul(
                    p23[:, 512 * c:512 * (c + 1)], xnT[:, 128 * d:128 * (d + 1)],
                    wA[:, D * d + 512 * c:D * d + 512 * (c + 1)],
                    start=(d == 0), stop=(d == 7)))
        pe_inc()
        mark("q_%d" % t, "s_pe")
        wait_ev("vector", "q_%d" % t)
        op("vector", lambda: nc.vector.tensor_copy(qsb, p23))
        mark("qsb_%d" % t, "s_dve")
        if t >= 1:
            wait_ev("scalar_dma", "at_%d" % (t - 1))
        wait_ev("scalar_dma", "qsb_%d" % t)
        xpose_dma(qTb, qsb, "s_tq")
        ntp["s_tq"] += 1

        if t + 1 < T2:
            s2_scalar(t + 1)

        def emit_ssq():
            for fc in range(4):
                wait_ev("vector", "sq_%d_%d" % (t, fc))
            op("vector", lambda: nc.vector.reduce_sum(n1, ssq4, AX.X))
            op("vector", lambda: nc.vector.tensor_scalar_mul(n1, n1, 1.0 / FF))
            newton_rsqrt(n1, n2, n3, n4, iters=2)
            op("vector", lambda: nc.vector.tensor_tensor(n3, n1, n2, ALU.mult))
            op("vector", lambda: nc.vector.tensor_scalar_add(n3, n3, 1e-8))
            op("vector", lambda: nc.vector.reciprocal(n4, n3))
            op("vector", lambda: nc.vector.tensor_scalar_mul(sct, n4, 0.25))
            mark("sct_%d" % t, "s_dve")

        def emit_down(fc):
            if fc == 0:
                if t >= 1:
                    wait_ev("tensor", "g_%d" % (t - 1))
                else:
                    wait_ev("tensor", "terndn_done")
                    wait_ev("tensor", "terndn_done_act")
            wait("tensor", "s_th%d" % (fc % 2), 16 * (2 * t + fc // 2 + 1))
            for k in range(8):
                Fj = 8 * fc + k
                st = (Fj == 0)
                sp = (Fj == 31)
                for c in range(2):
                    op("tensor", lambda Fj=Fj, k=k, fc=fc, c=c, st=st, sp=sp: nc.tensor.matmul(
                        p67[:, 512 * c:512 * (c + 1)],
                        hTfc[fc % 2][:, 128 * k:128 * (k + 1)],
                        wdn2[:, 1024 * Fj + 512 * c:1024 * Fj + 512 * (c + 1)],
                        start=st, stop=sp))
            pe_inc()
            mark("dwn_%d_%d" % (t, fc), "s_pe")

        emit_down(0)
        emit_hxpose(2)
        emit_down(1)
        emit_hxpose(3)
        emit_down(2)
        wait("tensor", "s_tq", 16 * ntp["s_tq"])
        wait_ev("tensor", "qsb_%d" % t)
        for h in range(NH):
            rb = 64 * (h % 2)
            fo = h // 2
            op("tensor", lambda h=h, rb=rb, fo=fo: nc.tensor.matmul(
                p23[rb:rb + 64, 128 * fo:128 * (fo + 1)],
                kvsb[rb:rb + 64, 64 * fo:64 * fo + 64],
                qTb[rb:rb + 64, 128 * fo:128 * (fo + 1)],
                start=True, stop=True))
        pe_inc()
        mark("at_%d" % t, "s_pe")
        wait_ev("vector", "at_%d" % t)
        op("vector", lambda: nc.vector.tensor_copy(aTb, p23))
        mark("aT_%d" % t, "s_dve")
        emit_ssq()
        emit_down(3)
        ev["down_%d" % t] = ev["dwn_%d_%d" % (t, 3)]

        wait_ev("scalar", "down_%d" % t)
        wait_ev("scalar", "sct_%d" % t)
        if t >= 1:
            wait("scalar", "s_out", 16 * nstores[0])
            wait_ev("scalar", "g_%d" % (t - 1))
        op("scalar", lambda: nc.scalar.activation(t1s, p67, AF.Tanh, scale=sct))
        mark("t2_%d" % t, "s_act")
        wait_ev("vector", "t2_%d" % t)
        op("vector", lambda: nc.vector.tensor_scalar_add(t1s, t1s, 1.0))
        op("vector", lambda: nc.vector.scalar_tensor_tensor(
            gs, p67, sct, t1s, ALU.mult, ALU.mult))
        mark("g_%d" % t, "s_dve")

        wait_ev("tensor", "aT_%d" % t)
        wait_ev("tensor", "h2_%d_3" % t)
        if t >= 1:
            wait_ev("tensor", "m1_%d" % (t - 1))
        for fo in range(8):
            st = (fo == 0)
            sp = (fo == 7)
            for c in range(2):
                op("tensor", lambda fo=fo, c=c, st=st, sp=sp: nc.tensor.matmul(
                    p01[:, 512 * c:512 * (c + 1)], aTb[:, 128 * fo:128 * (fo + 1)],
                    wB[:, D * fo + 512 * c:D * fo + 512 * (c + 1)],
                    start=st, stop=sp))
        pe_inc()
        mark("s_%d" % t, "s_pe")

        wait_ev("vector", "s_%d" % t)
        wait_ev("vector", "g_%d" % t)
        if t >= 1:
            wait_ev("vector", "m2_%d" % (t - 1))
        op("vector", lambda: nc.vector.scalar_tensor_tensor(
            tmp, p01, 1.0, gs, ALU.mult, ALU.mult))
        mark("m1_%d" % t, "s_dve")
        wait_ev("gpsimd_c", "m1_%d" % t)
        wait("gpsimd_c", "s_out", 16 * nstores[0])
        op("gpsimd_c", lambda pb=pb: nc.gpsimd.tensor_tensor(outs, tmp, xbuf[pb], ALU.add))
        mark("m2_%d" % t, "s_pool")
        wait_ev("sync", "m2_%d" % t)
        op("sync", lambda t=t: nc.sync.dma_start(
            outd[128 * t:128 * (t + 1), :], outs), inc="s_out", n=16)
        nstores[0] += 1

        if t + 2 < T2:
            load_x(t + 2, pb)

    wait("sync", "s_out", 16 * nstores[0])

    def runner(engobj, key):
        for item in ops[key]:
            if item[0] == "wait":
                _, semname, val = item
                if val > 0:
                    engobj.wait_ge(sems[semname], val)
            else:
                _, fn, inc, n = item
                inst = fn()
                if inc is not None:
                    inst.then_inc(sems[inc], n)

    nc._dbg_ops = {k: [(it[0], it[1], it[2]) if it[0] == "wait" else
                        ("op", it[1].__code__.co_firstlineno, it[2])
                       for it in v] for k, v in ops.items()}

    with nc.Block() as block:
        @block.sync
        def _(sync):
            runner(sync, "sync")

        @block.gpsimd
        def _(gpsimd):
            runner(gpsimd, "gpsimd")

        @block.vector
        def _(vector):
            runner(vector, "vector")

        @block.scalar
        def _(scalar):
            runner(scalar, "scalar")

        @block.tensor
        def _(tensor):
            runner(tensor, "tensor")

    return nc


_NC_CACHE = []


def kernel(L0_attn_x, L0_attn_W_q, L0_attn_W_k, L0_attn_W_v, L0_attn_W_o,
           L0_ffn_w_up, L0_ffn_w_down):
    x = np.ascontiguousarray(np.asarray(L0_attn_x, dtype=np.float32))
    wqT = np.ascontiguousarray(np.asarray(L0_attn_W_q, dtype=np.float32).T)
    wkT = np.ascontiguousarray(np.asarray(L0_attn_W_k, dtype=np.float32).T)
    wvT = np.ascontiguousarray(np.asarray(L0_attn_W_v, dtype=np.float32).T)
    woT = np.ascontiguousarray(np.asarray(L0_attn_W_o, dtype=np.float32).T)
    wupT = np.ascontiguousarray(np.asarray(L0_ffn_w_up, dtype=np.float32).T)
    wdnT = np.ascontiguousarray(np.asarray(L0_ffn_w_down, dtype=np.float32).T)

    in_maps = []
    for c in range(NCHIP):
        b, half = c // 2, c % 2
        xb = x[b]
        own = xb[2048 * half:2048 * (half + 1)]
        oth = xb[2048 * (1 - half):2048 * (2 - half)]
        xkv = np.ascontiguousarray(np.concatenate([own, oth], axis=0))
        in_maps.append({"xkv": xkv, "wqT": wqT, "wkT": wkT, "wvT": wvT,
                        "woT": woT, "wupT": wupT, "wdnT": wdnT})

    if not _NC_CACHE:
        _NC_CACHE.append(build_nc())
    nc = _NC_CACHE[0]
    res = run_bass_kernel_spmd(nc, in_maps, core_ids=list(range(NCHIP)))

    out = np.empty((4, 4096, 1024), np.float32)
    for c in range(NCHIP):
        b, half = c // 2, c % 2
        out[b, 2048 * half:2048 * (half + 1)] = res.results[c]["out"]
    return out
